# version 06 (final)
# baseline (speedup 1.0000x reference)
import sys
if "/opt/trn_rl_repo" not in sys.path:
    sys.path.insert(0, "/opt/trn_rl_repo")
import contextlib
import numpy as np
import ml_dtypes

import concourse.bass as bass
import concourse.mybir as mybir
from concourse.bass_utils import run_bass_kernel_spmd

F32 = mybir.dt.float32
BF16 = mybir.dt.bfloat16
AF = mybir.ActivationFunctionType
ALU = mybir.AluOpType
AX = mybir.AxisListType.X

D = 1024
DI = 2048
CSL = 512
NCB = 4
NST = 16
DTR = 64
EPS = 1e-5
RG = [[0, 1, 2, 3], [4, 5, 6, 7]]


class Plan:
    def __init__(self):
        self.ops = []
        self.cnt = {}

    def _e(self, s):
        if s not in self.cnt:
            self.cnt[s] = 0

    def at(self, s):
        self._e(s)
        return self.cnt[s]

    def op(self, eng, fn, *args, w=(), inc=(), **kw):
        rec = dict(eng=eng, fn=fn, args=args, kw=kw, waits=[], incs=[])
        for s, v in w:
            self._e(s)
            assert v <= self.cnt[s], f"wait beyond plan: {s} {v} > {self.cnt[s]} ({fn})"
            if v > 0:
                rec["waits"].append((s, v))
        for s, d in inc:
            self._e(s)
            self.cnt[s] += d
            rec["incs"].append((s, d))
        self.ops.append(rec)
        return rec

    def dma(self, qeng, out, in_, sem, w=(), **kw):
        return self.op(qeng, "dma_start", w=w, inc=((sem, 16),), out=out, in_=in_, **kw)

    def barrier(self, engines, sems):
        for e in engines:
            w = [(s, self.at(s)) for s in sems if self.at(s) > 0]
            if w:
                self.op(e, "_wait", w=w)


def build_program(L=8192, debug=False):
    TTA = L // 8
    TTB = L // 8
    NA = 8
    NB = 8
    AGW = L // 4
    NAG = 4
    TQ = L // 4
    CH = TQ // 2
    NTH = CH // 128
    FA = min(512, TTA)
    NFA = TTA // FA
    FB = min(512, TTB)
    NFB = TTB // FB
    FD = min(512, AGW)
    NFD = AGW // FD

    nc = bass.Bass("TRN2", target_bir_lowering=False, detect_race_conditions=False)

    io = {}
    def inp(name, shape, dt=F32):
        io[name] = nc.dram_tensor(name, list(shape), dt, kind="ExternalInput").ap()
        return io[name]

    x_full = inp("x_b", (L, D))
    x_res = inp("x_res", (TQ, D))
    w_in = inp("w_in", (D, 2 * CSL))
    conv_w = inp("conv_w", (CSL, 4))
    conv_b = inp("conv_b", (CSL, 1))
    w_xp = inp("w_xp", (CSL, 96))
    w_dt = inp("w_dt", (DTR, CSL))
    dt_bias = inp("dt_bias", (CSL, 1))
    a_log = inp("a_log", (CSL, NST))
    d_skip = inp("d_skip", (CSL, 1))
    w_out = inp("w_out", (DI, D))
    ws1 = inp("ws1", (D, 512))
    ws2 = inp("ws2", (512, D))
    w1 = inp("w1", (8 * D, 512))
    w2 = inp("w2", (8 * 512, D))
    wr = inp("wr", (D, 8))
    ident = inp("ident_bf", (128, 128), BF16)
    coreq = inp("coreq", (1, 1), mybir.dt.int32)

    out = nc.dram_tensor("out", [TQ, D], F32, kind="ExternalOutput").ap()
    dbg = {}
    if debug:
        for nm, shape, dt_ in [("dbg_u", (CSL, L), F32), ("dbg_delta", (CSL, L), F32),
                               ("dbg_yg", (CSL, L), BF16), ("dbg_xdbl", (96, L), F32),
                               ("dbg_x1n", (TQ, D), F32), ("dbg_prob", (TQ, 8), F32),
                               ("dbg_acc", (TQ, D), F32), ("dbg_h2", (TQ, 512), F32),
                               ("dbg_sq", (TQ, D), F32), ("dbg_cols", (TQ, 8), F32),
                               ("dbg_sp", (128, 1536), F32)]:
            dbg[nm] = nc.dram_tensor(nm, list(shape), dt_, kind="ExternalOutput").ap()

    u_dram = nc.dram_tensor("u_dram", [CSL, L], F32)
    zs_dram = nc.dram_tensor("zs_dram", [CSL, L], F32)
    delta_dram = nc.dram_tensor("delta_dram", [CSL, L], F32)
    bc_dram = nc.dram_tensor("bc_dram", [32, L], F32)
    xdbl_in = [nc.dram_tensor(f"xdbl_in{i}", [96, AGW], F32) for i in range(NAG)]
    xdbl_out = [nc.dram_tensor(f"xdbl_out{i}", [384, AGW], F32) for i in range(NAG)]
    yg_in = [nc.dram_tensor(f"yg_in{j}", [CSL, TTB], BF16) for j in range(NB)]
    yg_out = nc.dram_tensor("yg_out", [NB * DI, TTB], BF16)
    wffn_bf = nc.dram_tensor("wffn_bf", [9 * (D + 512), D], BF16)
    wout_bfd = nc.dram_tensor("wout_bf", [DI, D], BF16)

    es = contextlib.ExitStack()

    SB_TOP = 228352
    class Bump:
        def __init__(self, base):
            self.off = base
        def take(self, nbytes):
            o = (self.off + 31) // 32 * 32
            self.off = o + nbytes
            assert self.off <= SB_TOP, f"sbuf pool overflow at {self.off}"
            return o
    pers = Bump(16512)
    def sal(bump, name, shape, dt=F32):
        nbytes = int(np.prod(shape[1:])) * (2 if dt == BF16 else 4)
        if dt == mybir.dt.int32:
            nbytes = int(np.prod(shape[1:])) * 4
        return nc.alloc_sbuf_tensor_at(name, list(shape), dt, offset=bump.take(nbytes))

    identb = sal(pers, "identb", (128, 128), BF16)
    acol = [sal(pers, f"acol{c}", (128, NST)) for c in range(NCB)]
    dskip = [sal(pers, f"dskip{c}", (128, 1)) for c in range(NCB)]
    wrbf = [sal(pers, f"wrbf{k}", (128, 8), BF16) for k in range(8)]
    qcol = sal(pers, "qcol", (1, 1), mybir.dt.int32)
    cst = sal(pers, "cst", (128, 4))
    wstage = sal(pers, "wstage", (128, D))
    wstage_bf = sal(pers, "wstage_bf", (128, D), BF16)
    POOL = pers.off + 64

    pa = Bump(POOL)
    winbf = [sal(pa, f"winbf{k}", (128, 2 * CSL), BF16) for k in range(8)]
    wxpbf = [sal(pa, f"wxpbf{c}", (128, 96), BF16) for c in range(NCB)]
    wdtbf = sal(pa, "wdtbf", (DTR, CSL), BF16)
    convw = [sal(pa, f"convw{c}", (128, 4)) for c in range(NCB)]
    convb = [sal(pa, f"convb{c}", (128, 1)) for c in range(NCB)]
    dtb = [sal(pa, f"dtb{c}", (128, 1)) for c in range(NCB)]
    xrow = [sal(pa, f"xrow{i}", (128, D)) for i in range(2)]
    xrow_bf = [sal(pa, f"xrowbf{i}", (128, D), BF16) for i in range(2)]
    xTall = sal(pa, "xTall", (128, 8 * TTA), BF16)
    xp_ext = [[sal(pa, f"xp{c}_{r}", (128, TTA + 8)) for r in range(2)] for c in range(NCB)]
    cvA = [sal(pa, f"cvA{c}", (128, TTA)) for c in range(2)]
    cvB = [sal(pa, f"cvB{c}", (128, TTA)) for c in range(2)]
    u_buf = [sal(pa, f"ubuf{c}", (128, TTA)) for c in range(NCB)]
    u_bfb = [sal(pa, f"ubf{c}", (128, TTA), BF16) for c in range(NCB)]
    zs_st = [sal(pa, f"zsst{c}", (128, TTA)) for c in range(NCB)]
    xdbl_st = sal(pa, "xdblst", (96, TTA))
    gth = [sal(pa, f"gth{r}", (96, AGW)) for r in range(2)]
    xdbl_sum = sal(pa, "xdblsum", (96, AGW))
    dtr_bf = sal(pa, "dtrbf", (DTR, AGW), BF16)
    spa = [sal(pa, f"spa{i}", (128, FD)) for i in range(4)]
    dstage = sal(pa, "dstage", (128, AGW))

    pb = Bump(POOL)
    dl_t = [[sal(pb, f"dlt{c}_{r}", (128, TTB)) for r in range(2)] for c in range(NCB)]
    u_t = [[sal(pb, f"ut{c}_{r}", (128, TTB)) for r in range(2)] for c in range(NCB)]
    zs_t = [[sal(pb, f"zst{c}_{r}", (128, TTB)) for r in range(2)] for c in range(NCB)]
    du_t = [sal(pb, f"dut{c}", (128, TTB)) for c in range(NCB)]
    brep = [sal(pb, f"brep{r}", (128, TTB)) for r in range(2)]
    crep = [sal(pb, f"crep{r}", (128, TTB)) for r in range(2)]
    dA_b = [sal(pb, f"dAb{r}", (128, TTB)) for r in range(3)]
    dBu_b = [sal(pb, f"dBub{r}", (128, TTB)) for r in range(3)]
    h_b = [sal(pb, f"hb{r}", (128, TTB)) for r in range(3)]
    tmp_b = [sal(pb, f"tmpb{r}", (128, TTB), BF16) for r in range(4)]
    state = sal(pb, "state", (128, NCB * NST))
    y1_b = [sal(pb, f"y1b{r}", (128, TTB)) for r in range(2)]
    yg_b = [sal(pb, f"ygb{r}", (128, TTB), BF16) for r in range(2)]

    pc = Bump(POOL)
    yfull = sal(pc, "yfull", (128, 16 * TTB), BF16)
    woutb = [sal(pc, f"woutb{k}", (128, D), BF16) for k in range(16)]
    x1T = [sal(pc, f"x1T{k}", (128, CH), BF16) for k in range(8)]
    acc = [sal(pc, f"accc{i}", (128, D)) for i in range(NTH)]
    xres_t = [sal(pc, f"xrest{r}", (128, D)) for r in range(2)]
    x1n_t = [sal(pc, f"x1n{r}", (128, D)) for r in range(2)]
    x1b_t = [sal(pc, f"x1b{r}", (128, D), BF16) for r in range(2)]
    sq_t = [sal(pc, f"sqt{r}", (128, D)) for r in range(2)]
    col = [sal(pc, f"colx{i}", (128, 1)) for i in range(8)]
    s8 = sal(pc, "s8", (128, 8))
    logit_t = sal(pc, "logit", (128, 8))
    prob_t = [sal(pc, f"prob{i}", (128, 8)) for i in range(NTH)]
    w1e = [[sal(pc, f"w1e{r}_{k}", (128, 512), BF16) for k in range(8)] for r in range(2)]
    w2e = [[sal(pc, f"w2e{r}_{k}", (128, D), BF16) for k in range(4)] for r in range(2)]
    hrel = [sal(pc, f"hrel{r}", (128, 512)) for r in range(2)]
    h2tm = [sal(pc, f"h2tm{r}", (128, 512), BF16) for r in range(2)]
    h2T = [sal(pc, f"h2T{r}", (128, 512), BF16) for r in range(2)]
    xo_t = [sal(pc, f"xo{r}", (128, D)) for r in range(2)]

    psA = es.enter_context(nc.psum_tensor("psA", [128, 2048], F32))
    psB = es.enter_context(nc.psum_tensor("psB", [128, 2048], F32))

    P = Plan()
    ALL_ENG = ["sp", "dve", "act", "te", "gp"]

    P.dma("sp", identb[:], ident, "s_wld")
    P.dma("sp", qcol[:], coreq, "s_wld")
    P.op("dve", "memset", cst[:, 0:1], EPS, inc=(("s_cst", 1),))
    P.op("dve", "memset", cst[:, 1:2], -0.5, inc=(("s_cst", 1),))
    P.op("dve", "memset", cst[:, 2:3], -1.0, inc=(("s_cst", 1),))

    def ldcast(dst_bf, src_ap, width):
        P.dma("sp", wstage[:, 0:width], src_ap, "s_wsl",
              w=(("s_wcast", P.at("s_wcast")),))
        P.op("act", "activation", dst_bf, wstage[:, 0:width], AF.Copy,
             w=(("s_wsl", P.at("s_wsl")),), inc=(("s_wcast", 1),))

    for k in range(8):
        ldcast(winbf[k][:], w_in[k * 128:(k + 1) * 128, :], 2 * CSL)
    for c in range(NCB):
        ldcast(wxpbf[c][:], w_xp[c * 128:(c + 1) * 128, :], 96)
    P.dma("sp", wstage[0:DTR, 0:CSL], w_dt, "s_wsl", w=(("s_wcast", P.at("s_wcast")),))
    P.op("act", "activation", wdtbf[:], wstage[0:DTR, 0:CSL], AF.Copy,
         w=(("s_wsl", P.at("s_wsl")),), inc=(("s_wcast", 1),))
    for k in range(8):
        ldcast(wrbf[k][:], wr[k * 128:(k + 1) * 128, :], 8)
    for c in range(NCB):
        rs = slice(c * 128, (c + 1) * 128)
        P.dma("sp", convw[c][:], conv_w[rs, :], "s_wld")
        P.dma("sp", convb[c][:], conv_b[rs, :], "s_wld")
        P.dma("sp", dtb[c][:], dt_bias[rs, :], "s_wld")
        P.dma("sp", dskip[c][:], d_skip[rs, :], "s_wld")
        P.dma("sp", wstage[0:128, 0:NST], a_log[rs, :], "s_wsl",
              w=(("s_acol", P.at("s_acol")), ("s_wcast", P.at("s_wcast"))))
        P.op("act", "activation", acol[c][:], wstage[0:128, 0:NST], AF.Exp,
             w=(("s_wsl", P.at("s_wsl")),), inc=(("s_wcast", 1),))
        P.op("dve", "tensor_scalar_mul", acol[c][:], acol[c][:], -1.0,
             w=(("s_wcast", P.at("s_wcast")),), inc=(("s_acol", 1),))

    def precast(nrows, src_ap, dst_tensor, dst_row0, width):
        for k in range(nrows // 128):
            P.dma("sp", wstage[:, 0:width], src_ap[k * 128:(k + 1) * 128, :], "s_wsl",
                  w=(("s_wcast", P.at("s_wcast")),))
            P.op("act", "activation", wstage_bf[:, 0:width], wstage[:, 0:width], AF.Copy,
                 w=(("s_wsl", P.at("s_wsl")), ("s_wst", P.at("s_wst"))),
                 inc=(("s_wcast", 1),))
            P.dma("sp", dst_tensor.ap()[dst_row0 + k * 128: dst_row0 + (k + 1) * 128, 0:width],
                  wstage_bf[:, 0:width], "s_wst",
                  w=(("s_wcast", P.at("s_wcast")),))
    for e in range(8):
        precast(D, w1[e * D:(e + 1) * D, :], wffn_bf, e * (D + 512), 512)
        precast(512, w2[e * 512:(e + 1) * 512, :], wffn_bf, e * (D + 512) + D, D)
    precast(D, ws1, wffn_bf, 8 * (D + 512), 512)
    precast(512, ws2, wffn_bf, 8 * (D + 512) + D, D)
    precast(DI, w_out, wout_bfd, 0, D)
    P.op("sp", "_wait", w=(("s_wld", P.at("s_wld")),))
    P.op("sp", "nop", inc=(("s_prold", 1),))

    for c in range(NCB):
        for r in range(2):
            P.op("dve", "memset", xp_ext[c][r][:, 0:3], 0.0, inc=(("s_conv", 1),))

    ps_xz = [psA[:, i * 512:(i + 1) * 512] for i in range(4)]
    ps_tr = psB[:, 0:512].bitcast(BF16)
    ps_xd = psB[:, 512:1024]
    ps_dl = [psB[:, 1024:1536], psB[:, 1536:2048]]

    for j in range(NA):
        r = j % 2
        NTR = TTA // 128
        for i in range(NTR):
            ri = i % 2
            P.dma("sp", xrow[ri][:],
                  x_full[j * TTA + i * 128: j * TTA + (i + 1) * 128, :],
                  f"s_xl{ri}", w=(("s_xcast", max(0, P.at("s_xcast") - 1)),))
            P.op("act", "activation", xrow_bf[ri][:], xrow[ri][:], AF.Copy,
                 w=((f"s_xl{ri}", P.at(f"s_xl{ri}")), ("s_tr", max(0, P.at("s_tr") - 1))),
                 inc=(("s_xcast", 1),))
            for k in range(8):
                P.op("te", "transpose", ps_tr[:, k * 128:(k + 1) * 128],
                     xrow_bf[ri][:, k * 128:(k + 1) * 128], identb[:],
                     w=((("s_xcast", P.at("s_xcast")), ("s_xt", P.at("s_xt")),
                         ("s_prold", 1)) if k == 0 else ()),
                     inc=((("s_tr", 1),) if k == 7 else ()))
            P.op("act", "activation",
                 xTall[:].rearrange("p (k t) -> p k t", k=8)[:, :, i * 128:(i + 1) * 128],
                 ps_tr[:].rearrange("p (k t) -> p k t", k=8),
                 AF.Copy,
                 w=(("s_tr", P.at("s_tr")), ("s_xz", P.at("s_xz"))),
                 inc=(("s_xt", 1),))
        slot = 0
        for m in range(8):
            for f in range(NFA):
                sl = ps_xz[slot][:, 0:FA]
                for k in range(8):
                    P.op("te", "matmul", sl, winbf[k][:, m * 128:(m + 1) * 128],
                         xTall[:, k * TTA + f * FA: k * TTA + (f + 1) * FA],
                         w=((("s_xt", P.at("s_xt")),
                             ("s_xze", max(0, P.at("s_xze") - 3)),
                             ("s_wcast", P.at("s_wcast"))) if k == 0 else ()),
                         start=(k == 0), stop=(k == 7),
                         inc=((("s_xz", 1),) if k == 7 else ()))
                if m < 4:
                    P.op("act", "activation", xp_ext[m][r][:, 3 + f * FA: 3 + (f + 1) * FA],
                         sl, AF.Copy,
                         w=(("s_xz", P.at("s_xz")), ("s_conv", max(0, P.at("s_conv") - 4))),
                         inc=(("s_xze", 1),))
                else:
                    P.op("act", "activation", zs_st[m - 4][:, f * FA:(f + 1) * FA],
                         sl, AF.Silu,
                         w=(("s_xz", P.at("s_xz")), ("s_zsw", P.at("s_zsw"))),
                         inc=(("s_xze", 1),))
                slot = (slot + 1) % 4
        for c in range(NCB):
            P.dma("act", zs_dram.ap()[c * 128:(c + 1) * 128, j * TTA:(j + 1) * TTA],
                  zs_st[c][:], "s_zsw", w=(("s_xze", P.at("s_xze")),))
        for c in range(NCB):
            if j > 0:
                P.op("dve", "tensor_copy", xp_ext[c][r][:, 0:3],
                     xp_ext[c][1 - r][:, TTA:TTA + 3],
                     w=(("s_xze", P.at("s_xze")),))
            xe = xp_ext[c][r]
            ca, cb = cvA[c % 2], cvB[c % 2]
            P.op("dve", "tensor_scalar_mul", ca[:], xe[:, 0:TTA], convw[c][:, 0:1],
                 w=(("s_xze", P.at("s_xze")), ("s_prold", 1),
                    ("s_u", max(0, P.at("s_u") - 2))))
            P.op("dve", "scalar_tensor_tensor", cb[:], xe[:, 1:1 + TTA], convw[c][:, 1:2],
                 ca[:], ALU.mult, ALU.add)
            P.op("dve", "scalar_tensor_tensor", ca[:], xe[:, 2:2 + TTA], convw[c][:, 2:3],
                 cb[:], ALU.mult, ALU.add)
            P.op("dve", "scalar_tensor_tensor", cb[:], xe[:, 3:3 + TTA], convw[c][:, 3:4],
                 ca[:], ALU.mult, ALU.add, inc=(("s_conv", 1),))
            P.op("act", "activation", u_buf[c][:], cb[:], AF.Silu, bias=convb[c][:, 0:1],
                 w=(("s_conv", P.at("s_conv")), ("s_ubf", P.at("s_ubf")),
                    ("s_uw", P.at("s_uw"))),
                 inc=(("s_u", 1),))
            P.op("dve", "tensor_copy", u_bfb[c][:], u_buf[c][:],
                 w=(("s_u", P.at("s_u")), ("s_xd", P.at("s_xd"))), inc=(("s_ubf", 1),))
            P.dma("act", u_dram.ap()[c * 128:(c + 1) * 128, j * TTA:(j + 1) * TTA],
                  u_buf[c][:], "s_uw", w=(("s_u", P.at("s_u")),))
            if debug:
                P.dma("act", dbg["dbg_u"][c * 128:(c + 1) * 128, j * TTA:(j + 1) * TTA],
                      u_buf[c][:], "s_uw", w=(("s_u", P.at("s_u")),))
        for f in range(NFA):
            for c in range(NCB):
                P.op("te", "matmul", ps_xd[0:96, 0:FA], wxpbf[c][:],
                     u_bfb[c][:, f * FA:(f + 1) * FA],
                     w=((("s_ubf", P.at("s_ubf")), ("s_xds", P.at("s_xds")),
                         ("s_wcast", P.at("s_wcast"))) if c == 0 else ()),
                     start=(c == 0), stop=(c == NCB - 1),
                     inc=((("s_xd", 1),) if c == NCB - 1 else ()))
            P.op("dve", "tensor_copy", xdbl_st[:, f * FA:(f + 1) * FA], ps_xd[0:96, 0:FA],
                 w=(("s_xd", P.at("s_xd")), ("s_xdw", P.at("s_xdw"))),
                 inc=(("s_xds", 1),))
        half_ix = j % (AGW // TTA)
        P.dma("sp", xdbl_in[j // (AGW // TTA)].ap()[:, half_ix * TTA:(half_ix + 1) * TTA],
              xdbl_st[:], "s_xdw", w=(("s_xds", P.at("s_xds")),))

        if (j + 1) % (AGW // TTA) == 0:
            ci = j // (AGW // TTA)
            P.op("gp", "collective_compute", "AllGather", ALU.bypass,
                 w=(("s_xdw", P.at("s_xdw")),), inc=(("s_agx", 1),),
                 replica_groups=RG,
                 ins=[xdbl_in[ci].ap().opt()],
                 outs=[xdbl_out[ci].ap().opt()])
            for rr in range(2):
                P.dma("sp", gth[rr][:], xdbl_out[ci].ap()[rr * 96:(rr + 1) * 96, :],
                      "s_gth", w=((("s_agx", ci + 1)), ("s_xdsum", P.at("s_xdsum"))))
            P.op("dve", "tensor_add", xdbl_sum[:], gth[0][:], gth[1][:],
                 w=(("s_gth", P.at("s_gth")), ("s_bcw", P.at("s_bcw")),
                    ("s_dtrbf", P.at("s_dtrbf"))),
                 inc=(("s_gfree", 1),))
            for rr in range(2):
                P.dma("sp", gth[rr][:], xdbl_out[ci].ap()[(2 + rr) * 96:(3 + rr) * 96, :],
                      "s_gth", w=((("s_agx", ci + 1)), ("s_gfree", P.at("s_gfree"))))
            P.op("dve", "tensor_add", gth[0][:], gth[0][:], gth[1][:],
                 w=(("s_gth", P.at("s_gth")),))
            P.op("dve", "tensor_add", xdbl_sum[:], xdbl_sum[:], gth[0][:],
                 inc=(("s_xdsum", 1),))
            if debug:
                P.dma("sp", dbg["dbg_xdbl"][:, ci * AGW:(ci + 1) * AGW], xdbl_sum[:],
                      "s_dbgw", w=(("s_xdsum", P.at("s_xdsum")),))
            P.dma("sp", bc_dram.ap()[:, ci * AGW:(ci + 1) * AGW], xdbl_sum[64:96, :],
                  "s_bcw", w=(("s_xdsum", P.at("s_xdsum")),))
            P.op("act", "activation", dtr_bf[:], xdbl_sum[0:64, :], AF.Copy,
                 w=(("s_xdsum", P.at("s_xdsum")), ("s_dlmm", P.at("s_dlmm"))),
                 inc=(("s_dtrbf", 1),))
            for un, (c, f) in enumerate([(c, f) for c in range(NCB) for f in range(NFD)]):
                sl = ps_dl[un % 2][:, 0:FD]
                P.op("te", "matmul", sl, wdtbf[:, c * 128:(c + 1) * 128],
                     dtr_bf[:, f * FD:(f + 1) * FD],
                     w=(("s_dtrbf", P.at("s_dtrbf")), ("s_sp4", max(0, P.at("s_sp4") - 3)),
                        ("s_wcast", P.at("s_wcast"))),
                     start=True, stop=True, inc=(("s_dlmm", 1),))
                a0, a1 = spa[(un % 2) * 2], spa[(un % 2) * 2 + 1]
                P.op("act", "activation", a0[:], sl, AF.Abs, bias=dtb[c][:, 0:1],
                     w=(("s_dlmm", P.at("s_dlmm")), ("s_dl", max(0, P.at("s_dl") - 1))),
                     inc=(("s_sp1", 1),))
                P.op("act", "activation", a1[:], sl, AF.Relu, bias=dtb[c][:, 0:1],
                     w=(("s_dl", max(0, P.at("s_dl") - 1)),),
                     inc=(("s_sp4", 1),))
                P.op("act", "activation", a0[:], a0[:], AF.Exp, scale=cst[:, 2:3],
                     w=(("s_sp1", P.at("s_sp1")), ("s_cst", P.at("s_cst"))), inc=(("s_sp3", 1),))
                P.op("act", "activation", a0[:], a0[:], AF.Ln, bias=1.0,
                     w=(("s_sp3", P.at("s_sp3")),), inc=(("s_sp4", 1),))
                P.op("dve", "tensor_add", dstage[:, f * FD:(f + 1) * FD], a0[:], a1[:],
                     w=(("s_sp4", P.at("s_sp4")), ("s_dlw", P.at("s_dlw"))),
                     inc=(("s_dl", 1),))
                if f == NFD - 1:
                    P.dma("sp", delta_dram.ap()[c * 128:(c + 1) * 128, ci * AGW:(ci + 1) * AGW],
                          dstage[:], "s_dlw", w=(("s_dl", P.at("s_dl")),))
                    if debug:
                        P.dma("sp", dbg["dbg_delta"][c * 128:(c + 1) * 128, ci * AGW:(ci + 1) * AGW],
                              dstage[:], "s_dbgw", w=(("s_dl", P.at("s_dl")),))

    P.barrier(ALL_ENG, ["s_xze", "s_xds", "s_dl", "s_xt", "s_sp4",
                        "s_uw", "s_zsw", "s_dlw", "s_bcw"])
    DBU_ENG = {0: "dve", 1: "dve", 2: "gp", 3: "gp"}
    ps_y = [psA[:, 0:1024], psA[:, 1024:2048], psB[:, 0:1024], psB[:, 1024:2048]]

    for j in range(NB):
        r = j % 2
        ci = (j * TTB) // AGW
        cof = (j * TTB) % AGW
        for c in range(NCB):
            P.dma("sp", dl_t[c][r][:],
                  delta_dram.ap()[c * 128:(c + 1) * 128, j * TTB:(j + 1) * TTB],
                  f"s_bdl{r}", w=(("s_dA", max(0, P.at("s_dA") - NCB * NST)),))
            P.dma("sp", u_t[c][r][:],
                  u_dram.ap()[c * 128:(c + 1) * 128, j * TTB:(j + 1) * TTB],
                  f"s_bul{r}", w=(("s_yg", max(0, P.at("s_yg") - NCB)),
                              ("s_du", max(0, P.at("s_du") - NCB))))
            P.dma("sp", zs_t[c][r][:],
                  zs_dram.ap()[c * 128:(c + 1) * 128, j * TTB:(j + 1) * TTB],
                  f"s_bzl{r}", w=(("s_yg", max(0, P.at("s_yg") - NCB)),))
        for c in range(NCB):
            P.op("dve", "tensor_mul", du_t[c][:], dl_t[c][r][:], u_t[c][r][:],
                 w=((f"s_bdl{r}", P.at(f"s_bdl{r}")), (f"s_bul{r}", P.at(f"s_bul{r}")),
                    ("s_dBu_dve", P.at("s_dBu_dve")), ("s_dBu_gp", P.at("s_dBu_gp"))),
                 inc=(("s_du", 1),))
        for n in range(NST):
            rb = n % 2
            P.dma("sp", brep[rb][:],
                  bc_dram.ap()[n:n + 1, ci * AGW + cof: ci * AGW + cof + TTB]
                  .partition_broadcast(128).squeeze(1),
                  f"s_brl{rb}", w=(("s_dBu_dve", max(0, P.at("s_dBu_dve") - NCB // 2)), ("s_dBu_gp", max(0, P.at("s_dBu_gp") - NCB // 2)),))
            P.dma("sp", crep[rb][:],
                  bc_dram.ap()[16 + n:17 + n, ci * AGW + cof: ci * AGW + cof + TTB]
                  .partition_broadcast(128).squeeze(1),
                  f"s_crl{rb}", w=(("s_tmp", max(0, P.at("s_tmp") - NCB)),))
            for c in range(NCB):
                it = n * NCB + c
                ra = it % 3
                rt = it % 4
                P.op("act", "activation", dA_b[ra][:], dl_t[c][r][:], AF.Exp,
                     scale=acol[c][:, n:n + 1],
                     w=((f"s_bdl{r}", P.at(f"s_bdl{r}")), ("s_scan", max(0, P.at("s_scan") - 1)),
                        ("s_acol", P.at("s_acol"))),
                     inc=(("s_dA", 1),))
                dbsem = "s_dBu_" + DBU_ENG[c]
                P.op(DBU_ENG[c], "tensor_mul", dBu_b[ra][:], du_t[c][:], brep[rb][:],
                     w=(("s_du", P.at("s_du")), (f"s_brl{rb}", P.at(f"s_brl{rb}")),
                        ("s_scan", max(0, P.at("s_scan") - 1))),
                     inc=((dbsem, 1),))
                init = 0.0 if j == 0 else state[:, c * NST + n: c * NST + n + 1]
                P.op("dve", "tensor_tensor_scan", h_b[ra][:], dA_b[ra][:], dBu_b[ra][:],
                     init, ALU.mult, ALU.add,
                     w=(("s_dA", P.at("s_dA")), (dbsem, P.at(dbsem)),
                        ("s_tmp", max(0, P.at("s_tmp") - 1)),
                        ("s_stuse", max(0, P.at("s_stuse") - 1))),
                     inc=(("s_scan", 1),))
                P.op("dve", "tensor_copy", state[:, c * NST + n: c * NST + n + 1],
                     h_b[ra][:, TTB - 1:TTB],
                     w=(("s_scan", P.at("s_scan")),), inc=(("s_stuse", 1),))
                P.op("dve", "tensor_mul", tmp_b[rt][:], h_b[ra][:], crep[rb][:],
                     w=((f"s_crl{rb}", P.at(f"s_crl{rb}")),
                        ("s_nred", max(0, P.at("s_nred") - 3))),
                     inc=(("s_tmp", 1),))
                for f in range(NFB):
                    P.op("te", "matmul", ps_y[c][:, f * FB:(f + 1) * FB], identb[:],
                         tmp_b[rt][:, f * FB:(f + 1) * FB],
                         w=((("s_tmp", P.at("s_tmp")), ("s_prold", 1),
                             ("s_yg", P.at("s_yg"))) if f == 0 else ()),
                         start=(n == 0), stop=(n == NST - 1),
                         inc=((("s_nred", 1),) if f == NFB - 1 else ()))
        for c in range(NCB):
            rg = c % 2
            P.op("dve", "scalar_tensor_tensor", y1_b[rg][:], u_t[c][r][:], dskip[c][:, 0:1],
                 ps_y[c][:, 0:TTB], ALU.mult, ALU.add,
                 w=(("s_nred", P.at("s_nred")), (f"s_bul{r}", P.at(f"s_bul{r}")),
                    ("s_prold", 1)))
            P.op("dve", "tensor_mul", yg_b[rg][:], y1_b[rg][:], zs_t[c][r][:],
                 w=((f"s_bzl{r}", P.at(f"s_bzl{r}")), (f"s_ygw{j % 2}", max(0, P.at(f"s_ygw{j % 2}") - 16))),
                 inc=(("s_yg", 1),))
            P.dma("act", yg_in[j].ap()[c * 128:(c + 1) * 128, :], yg_b[rg][:], f"s_ygw{j % 2}",
                  w=(("s_yg", P.at("s_yg")),))
            if debug:
                P.dma("sp", dbg["dbg_yg"][c * 128:(c + 1) * 128, j * TTB:(j + 1) * TTB],
                      yg_b[rg][:], "s_dbgw", w=(("s_yg", P.at("s_yg")),))
        P.op("gp", "collective_compute", "AllGather", ALU.bypass,
             w=((f"s_ygw{j % 2}", P.at(f"s_ygw{j % 2}")),), inc=(("s_agy", 1),),
             replica_groups=RG,
             ins=[yg_in[j].ap().opt()],
             outs=[yg_out.ap()[j * DI:(j + 1) * DI, :].opt()])

    P.barrier(ALL_ENG, ["s_yg", "s_nred", "s_scan", "s_tmp", "s_ygw0", "s_ygw1", "s_wst", "s_prold"])
    for k in range(16):
        P.dma("sp", woutb[k][:], wout_bfd.ap()[k * 128:(k + 1) * 128, :], "s_wob")

    ps_h = [psA[:, 0:512], psA[:, 512:1024]]
    ps_trp = psA[:, 0:512].bitcast(BF16)
    ps_r = psA[:, 512:520]
    ps_wo = psA[:, 1024:2048]
    ps_t2 = psA[:, 1024:1536].bitcast(BF16)
    ps_e = [psB[:, 0:1024], psB[:, 1024:2048]]

    for half in range(2):
        P.op("sp", "_yfull_load", half, yfull, yg_out, TTB,
             w=((("s_agy", 7 + half)), ("s_womm", P.at("s_womm"))),
             inc=(("s_yfl", 16),))
        for i in range(NTH):
            t0 = i * 128
            ri = i % 2
            for fh in range(2):
                for k in range(16):
                    P.op("te", "matmul", ps_wo[:, fh * 512:(fh + 1) * 512],
                         yfull[:, k * TTB + t0: k * TTB + t0 + 128],
                         woutb[k][:, fh * 512:(fh + 1) * 512],
                         w=((("s_yfl", P.at("s_yfl")), ("s_wob", P.at("s_wob")),
                             ("s_x1", P.at("s_x1")), ("s_h2", P.at("s_h2"))) if (k == 0 and fh == 0) else ()),
                         start=(k == 0), stop=(k == 15),
                         inc=((("s_womm", 1),) if (k == 15 and fh == 1) else ()))
            P.dma("sp", xres_t[ri][:], x_res[half * CH + t0: half * CH + t0 + 128, :],
                  f"s_xrl{ri}", w=(("s_x1", max(0, P.at("s_x1") - 1)),))
            P.op("dve", "tensor_add", x1n_t[ri][:], xres_t[ri][:], ps_wo,
                 w=(("s_womm", P.at("s_womm")), (f"s_xrl{ri}", P.at(f"s_xrl{ri}")),
                    ("s_acc", max(0, P.at("s_acc") - 1)),
                    ("s_x1b", P.at("s_x1b")), ("s_sq", P.at("s_sq"))),
                 inc=(("s_x1", 1),))
            P.op("dve", "tensor_reduce", col[0][:], x1n_t[ri][:], AX, ALU.add,
                 w=(("s_c2", max(0, P.at("s_c2") - 1)),), inc=(("s_c0", 1),))
            P.op("act", "activation", sq_t[ri][:], x1n_t[ri][:], AF.Square,
                 w=(("s_x1", P.at("s_x1")), ("s_c1", max(0, P.at("s_c1") - 1))),
                 inc=(("s_sq", 1),))
            P.op("dve", "tensor_reduce", col[1][:], sq_t[ri][:], AX, ALU.add,
                 w=(("s_sq", P.at("s_sq")), ("s_c4", max(0, P.at("s_c4") - 1))),
                 inc=(("s_c1", 1),))
            P.op("dve", "tensor_scalar_mul", col[2][:], col[0][:], 1.0 / D,
                 w=(("s_c0", P.at("s_c0")), ("s_x1n", max(0, P.at("s_x1n") - 1))),
                 inc=(("s_c2", 1),))
            P.op("dve", "tensor_scalar", col[3][:], col[2][:], col[2][:, 0:1], -1.0,
                 ALU.mult, ALU.mult,
                 w=(("s_c2", P.at("s_c2")), ("s_c4", max(0, P.at("s_c4") - 1))),
                 inc=(("s_c3", 1),))
            P.op("dve", "scalar_tensor_tensor", col[4][:], col[1][:], 1.0 / D, col[3][:],
                 ALU.mult, ALU.add,
                 w=(("s_c1", P.at("s_c1")), ("s_c3", P.at("s_c3")),
                    ("s_c5", max(0, P.at("s_c5") - 1))),
                 inc=(("s_c4", 1),))
            P.op("act", "activation", col[5][:], col[4][:], AF.Ln, bias=cst[:, 0:1],
                 w=(("s_c4", P.at("s_c4")), ("s_c6", max(0, P.at("s_c6") - 1)), ("s_cst", P.at("s_cst"))),
                 inc=(("s_c5", 1),))
            P.op("act", "activation", col[6][:], col[5][:], AF.Exp, scale=cst[:, 1:2],
                 w=(("s_c5", P.at("s_c5")), ("s_x1n", max(0, P.at("s_x1n") - 1))),
                 inc=(("s_c6", 1),))
            P.op("dve", "tensor_scalar", x1n_t[ri][:], x1n_t[ri][:], col[2][:, 0:1],
                 col[6][:, 0:1], ALU.subtract, ALU.mult,
                 w=(("s_c6", P.at("s_c6")),), inc=(("s_x1n", 1),))
            if debug:
                P.dma("sp", dbg["dbg_x1n"][half * CH + t0: half * CH + t0 + 128, :],
                      x1n_t[ri][:], "s_dbgw", w=(("s_x1n", P.at("s_x1n")),))
            P.op("act", "activation", x1b_t[ri][:], x1n_t[ri][:], AF.Copy,
                 w=(("s_x1n", P.at("s_x1n")), ("s_t2p", max(0, P.at("s_t2p") - 8))),
                 inc=(("s_x1b", 1),))
            for k in range(8):
                P.op("te", "transpose", ps_trp[:, 0:128], x1b_t[ri][:, k * 128:(k + 1) * 128],
                     identb[:],
                     w=(("s_x1b", P.at("s_x1b")), ("s_x1T", P.at("s_x1T")),
                        ("s_rel", P.at("s_rel"))),
                     inc=(("s_t2p", 1),))
                P.op("act", "activation", x1T[k][:, t0:t0 + 128], ps_trp[:, 0:128], AF.Copy,
                     w=(("s_t2p", P.at("s_t2p")), ("s_hmm", P.at("s_hmm")),
                        ("s_rmm", P.at("s_rmm"))),
                     inc=(("s_x1T", 1),))
            for k in range(8):
                P.op("te", "matmul", ps_r, x1T[k][:, t0:t0 + 128], wrbf[k][:],
                     w=((("s_x1T", P.at("s_x1T")), ("s_lg", P.at("s_lg")),
                         ("s_rel", P.at("s_rel"))) if k == 0 else ()),
                     start=(k == 0), stop=(k == 7),
                     inc=((("s_rmm", 1),) if k == 7 else ()))
            P.op("dve", "tensor_copy", logit_t[:], ps_r,
                 w=(("s_rmm", P.at("s_rmm")), ("s_m8", max(0, P.at("s_m8") - 1))),
                 inc=(("s_lg", 1),))
            P.op("dve", "max", s8[:], logit_t[:],
                 w=(("s_lg", P.at("s_lg")), ("s_d0", max(0, P.at("s_d0") - 1))),
                 inc=(("s_m8", 1),))
            P.op("dve", "tensor_scalar", col[7][:], s8[:, 1:2], s8[:, 0:1], None,
                 ALU.subtract, w=(("s_m8", P.at("s_m8")), ("s_d1", max(0, P.at("s_d1") - 1))),
                 inc=(("s_d0", 1),))
            P.op("act", "activation", col[7][:], col[7][:], AF.Exp,
                 w=(("s_d0", P.at("s_d0")), ("s_d2", max(0, P.at("s_d2") - 1))),
                 inc=(("s_d1", 1),))
            P.op("dve", "tensor_scalar", col[7][:], col[7][:], 1.0, 1.0, ALU.add,
                 ALU.is_equal, w=(("s_d1", P.at("s_d1")),), inc=(("s_d2", 1),))
            P.op("dve", "tensor_scalar", prob_t[i][:], logit_t[:], s8[:, 0:1],
                 col[7][:, 0:1], ALU.is_equal, ALU.mult,
                 w=(("s_d2", P.at("s_d2")), ("s_rel", P.at("s_rel"))),
                 inc=(("s_prob", 1),))
            if debug:
                P.dma("sp", dbg["dbg_prob"][half * CH + t0: half * CH + t0 + 128, :],
                      prob_t[i][:], "s_dbgw", w=(("s_prob", P.at("s_prob")),))
            P.op("dve", "tensor_copy", acc[i][:], x1n_t[ri][:],
                 w=(("s_xo", max(0, P.at("s_xo") - 1)),),
                 inc=(("s_acc", 1),))
        for e in range(9):
            re_ = e % 2
            r0 = e * (D + 512)
            for k in range(8):
                P.dma("sp", w1e[re_][k][:],
                      wffn_bf.ap()[r0 + k * 128: r0 + (k + 1) * 128, 0:512],
                      f"s_w1l{re_}", w=(("s_hmm", max(0, P.at("s_hmm") - NTH)),))
            for k in range(4):
                P.dma("sp", w2e[re_][k][:],
                      wffn_bf.ap()[r0 + D + k * 128: r0 + D + (k + 1) * 128, 0:D],
                      f"s_w2l{re_}", w=(("s_emm", max(0, P.at("s_emm") - NTH)),))
            for i in range(NTH):
                t0 = i * 128
                rh = i % 2
                for k in range(8):
                    P.op("te", "matmul", ps_h[rh], x1T[k][:, t0:t0 + 128], w1e[re_][k][:],
                         w=((("s_x1T", P.at("s_x1T")), (f"s_w1l{re_}", P.at(f"s_w1l{re_}")),
                             ("s_rel", max(0, P.at("s_rel") - 1)),
                             ("s_x1T", P.at("s_x1T")), ("s_lg", P.at("s_lg"))) if k == 0 else ()),
                         start=(k == 0), stop=(k == 7),
                         inc=((("s_hmm", 1),) if k == 7 else ()))
                pe = prob_t[i][:, e:e + 1] if e < 8 else 1.0
                P.op("dve", "tensor_scalar", hrel[rh][:], ps_h[rh], 0.0, pe,
                     ALU.max, ALU.mult,
                     w=(("s_hmm", P.at("s_hmm")), ("s_prob", P.at("s_prob")),
                        ("s_sq2", max(0, P.at("s_sq2") - 1))),
                     inc=(("s_rel", 1),))
                if debug and e == 8:
                    P.dma("sp", dbg["dbg_h2"][half * CH + i * 128: half * CH + (i + 1) * 128, :],
                          hrel[rh][:], "s_dbgw", w=(("s_rel", P.at("s_rel")),))
                P.op("act", "activation", h2tm[rh][:], hrel[rh][:], AF.Square,
                     w=(("s_rel", P.at("s_rel")), ("s_h2t", max(0, P.at("s_h2t") - 4))),
                     inc=(("s_sq2", 1),))
                for k in range(4):
                    P.op("te", "transpose", ps_t2[:, 0:128], h2tm[rh][:, k * 128:(k + 1) * 128],
                         identb[:],
                         w=(("s_sq2", P.at("s_sq2")), ("s_h2", P.at("s_h2")),
                            ("s_x1", P.at("s_x1"))),
                         inc=(("s_h2t", 1),))
                    P.op("act", "activation", h2T[rh][:, k * 128:(k + 1) * 128],
                         ps_t2[:, 0:128], AF.Copy,
                         w=(("s_h2t", P.at("s_h2t")), ("s_emm", max(0, P.at("s_emm") - 1))),
                         inc=(("s_h2", 1),))
                for fh in range(2):
                    for k in range(4):
                        P.op("te", "matmul", ps_e[rh][:, fh * 512:(fh + 1) * 512],
                             h2T[rh][:, k * 128:(k + 1) * 128],
                             w2e[re_][k][:, fh * 512:(fh + 1) * 512],
                             w=((("s_h2", P.at("s_h2")), (f"s_w2l{re_}", P.at(f"s_w2l{re_}")),
                                 ("s_acc", max(0, P.at("s_acc") - 1))) if (k == 0 and fh == 0) else ()),
                             start=(k == 0), stop=(k == 3),
                             inc=((("s_emm", 1),) if (k == 3 and fh == 1) else ()))
                P.op("dve", "tensor_add", acc[i][:], acc[i][:], ps_e[rh],
                     w=(("s_emm", P.at("s_emm")), ("s_acc", P.at("s_acc"))),
                     inc=(("s_acc", 1),))
        for i in range(NTH):
            ri = i % 2
            if debug:
                P.dma("sp", dbg["dbg_acc"][half * CH + i * 128: half * CH + (i + 1) * 128, :],
                      acc[i][:], "s_dbgw", w=(("s_acc", P.at("s_acc")),))
            P.op("dve", "tensor_reduce", col[0][:], acc[i][:], AX, ALU.add,
                 w=(("s_acc", P.at("s_acc")), ("s_c2", max(0, P.at("s_c2") - 1))),
                 inc=(("s_c0", 1),))
            P.op("act", "activation", sq_t[ri][:], acc[i][:], AF.Square,
                 w=(("s_c1", max(0, P.at("s_c1") - 1)), ("s_sq", P.at("s_sq")),
                    ("s_acc", P.at("s_acc"))),
                 inc=(("s_sq", 1),))
            P.op("dve", "tensor_reduce", col[1][:], sq_t[ri][:], AX, ALU.add,
                 w=(("s_sq", P.at("s_sq")), ("s_c4", max(0, P.at("s_c4") - 1))),
                 inc=(("s_c1", 1),))
            P.op("dve", "tensor_scalar_mul", col[2][:], col[0][:], 1.0 / D,
                 w=(("s_c0", P.at("s_c0")), ("s_xo", max(0, P.at("s_xo") - 1))),
                 inc=(("s_c2", 1),))
            P.op("dve", "tensor_scalar", col[3][:], col[2][:], col[2][:, 0:1], -1.0,
                 ALU.mult, ALU.mult,
                 w=(("s_c2", P.at("s_c2")), ("s_c4", max(0, P.at("s_c4") - 1))),
                 inc=(("s_c3", 1),))
            P.op("dve", "scalar_tensor_tensor", col[4][:], col[1][:], 1.0 / D, col[3][:],
                 ALU.mult, ALU.add,
                 w=(("s_c1", P.at("s_c1")), ("s_c3", P.at("s_c3")),
                    ("s_c5", max(0, P.at("s_c5") - 1))),
                 inc=(("s_c4", 1),))
            P.op("act", "activation", col[5][:], col[4][:], AF.Ln, bias=cst[:, 0:1],
                 w=(("s_c4", P.at("s_c4")), ("s_c6", max(0, P.at("s_c6") - 1)), ("s_cst", P.at("s_cst"))),
                 inc=(("s_c5", 1),))
            P.op("act", "activation", col[6][:], col[5][:], AF.Exp, scale=cst[:, 1:2],
                 w=(("s_c5", P.at("s_c5")), ("s_xo", max(0, P.at("s_xo") - 1))),
                 inc=(("s_c6", 1),))
            if debug:
                P.dma("sp", dbg["dbg_sq"][half * CH + i * 128: half * CH + (i + 1) * 128, :],
                      sq_t[ri][:], "s_dbgw", w=(("s_c1", P.at("s_c1")),))
                for cc in range(7):
                    P.op("dve", "tensor_copy", s8[:, cc:cc + 1], col[cc][:],
                         w=(("s_c6", P.at("s_c6")),), inc=(("s_dbgc", 1),))
                P.dma("sp", dbg["dbg_cols"][half * CH + i * 128: half * CH + (i + 1) * 128, :],
                      s8[:], "s_dbgw", w=(("s_dbgc", P.at("s_dbgc")),))
            P.op("dve", "tensor_scalar", xo_t[ri][:], acc[i][:], col[2][:, 0:1],
                 col[6][:, 0:1], ALU.subtract, ALU.mult,
                 w=(("s_c6", P.at("s_c6")), ("s_ow", max(0, P.at("s_ow") - 16))),
                 inc=(("s_xo", 1),))
            P.dma("sp", out[half * CH + i * 128: half * CH + (i + 1) * 128, :], xo_t[ri][:],
                  "s_ow", w=(("s_xo", P.at("s_xo")),))

    P.op("sp", "_wait", w=(("s_ow", P.at("s_ow")),))
    if debug:
        P.op("sp", "_wait", w=(("s_dbgw", P.at("s_dbgw")), ("s_uw", P.at("s_uw")),
                               ("s_dbgw", P.at("s_dbgw"))))

    sems = {s: es.enter_context(nc.semaphore(s)) for s in P.cnt}

    with nc.Block() as block:
        def make(engname):
            def fn(eng):
                jreg = {}
                for recd in P.ops:
                    if recd["eng"] != engname:
                        continue
                    for (s, v) in recd["waits"]:
                        eng.wait_ge(sems[s], v)
                    if recd["fn"] == "_wait":
                        continue
                    if recd["fn"] == "_yfull_load":
                        half, yf, ygo, ttb = recd["args"]
                        if "q" not in jreg:
                            jreg["q"] = eng.value_load(qcol[0:1, 0:1])
                        rt = eng.snap(jreg["q"] * 2 + half, min_val=0, max_val=NB - 1)
                        srcap = ygo.ap()[bass.ts(rt, DI), :].rearrange(
                            "(k p) t -> p k t", p=128)
                        dstap = yf[:].rearrange("p (k t) -> p k t", k=16)
                        ins = eng.dma_start(out=dstap, in_=srcap)
                        for (s, d) in recd["incs"]:
                            ins.then_inc(sems[s], d)
                        continue
                    ins = getattr(eng, recd["fn"])(*recd["args"], **recd["kw"])
                    for (s, d) in recd["incs"]:
                        ins.then_inc(sems[s], d)
            return fn
        block.sync(make("sp"))
        block.vector(make("dve"))
        block.scalar(make("act"))
        block.tensor(make("te"))
        block.gpsimd(make("gp"))
    es.close()
    return nc


def make_inputs(full, core, L=8192):
    b, q = core // 4, core % 4
    TQ = L // 4
    x = np.ascontiguousarray(full["L0_attn_x"][b, :L])
    cs = slice(q * CSL, (q + 1) * CSL)
    w_in_s = np.ascontiguousarray(
        np.concatenate([full["L0_attn_W_in"][:, q * CSL:(q + 1) * CSL],
                        full["L0_attn_W_in"][:, DI + q * CSL: DI + (q + 1) * CSL]], axis=1))
    return {
        "x_b": x,
        "x_res": np.ascontiguousarray(x[q * TQ:(q + 1) * TQ]),
        "w_in": w_in_s,
        "conv_w": np.ascontiguousarray(full["L0_attn_conv_w"][cs, 0, :]),
        "conv_b": np.ascontiguousarray(full["L0_attn_conv_b"][cs].reshape(CSL, 1)),
        "w_xp": np.ascontiguousarray(full["L0_attn_W_xp"][cs]),
        "w_dt": np.ascontiguousarray(full["L0_attn_W_dt"][:, cs]),
        "dt_bias": np.ascontiguousarray(full["L0_attn_dt_bias"][cs].reshape(CSL, 1)),
        "a_log": np.ascontiguousarray(full["L0_attn_A_log"][cs]),
        "d_skip": np.ascontiguousarray(full["L0_attn_D_skip"][cs].reshape(CSL, 1)),
        "w_out": np.ascontiguousarray(full["L0_attn_W_out"]),
        "ws1": np.ascontiguousarray(full["L0_ffn_Ws1"]),
        "ws2": np.ascontiguousarray(full["L0_ffn_Ws2"]),
        "w1": np.ascontiguousarray(full["L0_ffn_W1"].reshape(8 * D, 512)),
        "w2": np.ascontiguousarray(full["L0_ffn_W2"].reshape(8 * 512, D)),
        "wr": np.ascontiguousarray(full["L0_ffn_Wr"]),
        "ident_bf": np.eye(128, dtype=ml_dtypes.bfloat16),
        "coreq": np.array([[q]], dtype=np.int32),
    }


def kernel(**inputs):
    L = 8192
    nc = build_program(L=L, debug=False)
    in_maps = [make_inputs(inputs, c, L=L) for c in range(8)]
    res = run_bass_kernel_spmd(nc, in_maps, core_ids=list(range(8)))
    TQ = L // 4
    out = np.empty((2, L, D), dtype=np.float32)
    for c in range(8):
        b, q = c // 4, c % 4
        out[b, q * TQ:(q + 1) * TQ, :] = res.results[c]["out"]
    return out
